# version 07
# speedup vs baseline: 1.7441x; 1.0939x over previous
import numpy as np
from contextlib import ExitStack

import concourse.bass as bass
import concourse.mybir as mybir
from concourse.bass_utils import run_bass_kernel_spmd

F32 = mybir.dt.float32
BF16 = mybir.dt.bfloat16
AF = mybir.ActivationFunctionType
ALU = mybir.AluOpType
AX = mybir.AxisListType

D = 1024
T = 2048
NJC = 8
NTC = 16
NH = 16
HD = 64
EPS = 1e-5
NEG_L0 = -3.3895313892515355e38
NEG_L1 = -9.0e15

ENGS = ("sync", "tensor", "vector", "scalar", "gpsimd")


class Prog:

    def __init__(self, nc, ctx):
        self.nc = nc
        self.ops = {k: [] for k in ENGS}
        self.sems = {}
        self.cnt = {}
        for name in ("ld", "lda", "ld2", "w", "wa", "wb", "wi", "t", "v", "s", "g", "st"):
            self.sems[name] = ctx.enter_context(nc.semaphore("sem_" + name))
            self.cnt[name] = 0

    def add(self, eng, fn):
        self.ops[eng].append(fn)

    def wait(self, eng, sem, val):
        if val <= 0:
            return
        s = self.sems[sem]
        self.add(eng, lambda E, s=s, val=val: E.wait_ge(s, val))

    def wait_cur(self, eng, sem):
        self.wait(eng, sem, self.cnt[sem])

    def op(self, eng, fn, inc=None, amount=1):
        if inc is None:
            if eng in ("vector", "scalar", "gpsimd"):
                self.add(eng, lambda E: E.drain())
            self.add(eng, fn)
            return None
        self.cnt[inc] += amount
        s, tot = self.sems[inc], self.cnt[inc]
        if eng in ("vector", "scalar", "gpsimd"):
            self.add(eng, lambda E: E.drain())
        self.add(eng, lambda E, fn=fn, s=s, amount=amount: fn(E).then_inc(s, amount))
        return tot

    def drain(self, eng):
        self.add(eng, lambda E: E.drain())

    def emit(self):
        with self.nc.Block() as block:
            @block.sync
            def _(E):
                for f in self.ops["sync"]:
                    f(E)

            @block.tensor
            def _(E):
                for f in self.ops["tensor"]:
                    f(E)

            @block.vector
            def _(E):
                for f in self.ops["vector"]:
                    f(E)

            @block.scalar
            def _(E):
                for f in self.ops["scalar"]:
                    f(E)

            @block.gpsimd
            def _(E):
                for f in self.ops["gpsimd"]:
                    f(E)


def alloc_consts(nc, ctx):
    c = {}
    c["ident_bf"] = ctx.enter_context(nc.sbuf_tensor("ident_bf", [128, 128], BF16))
    c["mask0_bf"] = ctx.enter_context(nc.sbuf_tensor("mask0_bf", [128, 128], BF16))
    c["mask1_bf"] = ctx.enter_context(nc.sbuf_tensor("mask1_bf", [128, 128], BF16))
    c["ones_col_bf"] = ctx.enter_context(nc.sbuf_tensor("ones_col_bf", [1, 128], BF16))
    c["ones128_bf"] = ctx.enter_context(nc.sbuf_tensor("ones128_bf", [128, 1], BF16))
    c["ones_col_f32"] = ctx.enter_context(nc.sbuf_tensor("ones_col_f32", [1, 128], F32))
    c["ident_f32"] = ctx.enter_context(nc.sbuf_tensor("ident_f32", [128, 128], F32))
    c["eps_tile"] = ctx.enter_context(nc.sbuf_tensor("eps_tile", [1, 1], F32))
    c["ones2blk"] = ctx.enter_context(nc.sbuf_tensor("ones2blk", [128, 2], F32))
    return c


def emit_consts(p, c):
    def f(E):
        E.memset(c["ident_bf"][:], 0.0)
        E.drain()
        E.affine_select(out=c["ident_bf"][:], in_=c["ident_bf"][:],
                        compare_op=ALU.not_equal, fill=1.0, base=0,
                        pattern=[[-1, 128]], channel_multiplier=1)
        E.memset(c["mask0_bf"][:], 0.0)
        E.drain()
        E.affine_select(out=c["mask0_bf"][:], in_=c["mask0_bf"][:],
                        compare_op=ALU.is_ge, fill=NEG_L0, base=0,
                        pattern=[[-1, 128]], channel_multiplier=1)
        E.memset(c["mask1_bf"][:], 0.0)
        E.drain()
        E.affine_select(out=c["mask1_bf"][:], in_=c["mask1_bf"][:],
                        compare_op=ALU.is_ge, fill=NEG_L1, base=0,
                        pattern=[[-1, 128]], channel_multiplier=1)
        E.drain()
        E.memset(c["ident_f32"][:], 0.0)
        E.drain()
        E.affine_select(out=c["ident_f32"][:], in_=c["ident_f32"][:],
                        compare_op=ALU.not_equal, fill=1.0, base=0,
                        pattern=[[-1, 128]], channel_multiplier=1)
        E.drain()
        E.memset(c["ones_col_bf"][:], 1.0)
        E.memset(c["ones128_bf"][:], 1.0)
        E.memset(c["ones_col_f32"][:], 1.0)
        E.memset(c["ones2blk"][:], 1.0)
        E.drain()
        E.affine_select(out=c["ones2blk"][:], in_=c["ones2blk"][:],
                        compare_op=ALU.is_ge, fill=0.0, base=0,
                        pattern=[[-64, 2]], channel_multiplier=1)
        E.drain()
        E.affine_select(out=c["ones2blk"][:], in_=c["ones2blk"][:],
                        compare_op=ALU.is_ge, fill=0.0, base=63,
                        pattern=[[64, 2]], channel_multiplier=-1)
        E.drain()
        return E.memset(c["eps_tile"][:], EPS)
    return p.op("gpsimd", f, inc="g")


def ln_alloc(nc, ctx, b, need_r=True, sfx="", n_par=2):
    b["_npar"] = n_par
    for i in range(n_par):
        if need_r:
            b[f"scr_r{i}"] = ctx.enter_context(
                nc.sbuf_tensor(f"scr_r{i}" + sfx, [128, 1024], F32))
        b[f"scr_d{i}"] = ctx.enter_context(
            nc.sbuf_tensor(f"scr_d{i}" + sfx, [128, 1024], F32))
        b[f"rb_bf{i}"] = ctx.enter_context(
            nc.sbuf_tensor(f"rb_bf{i}" + sfx, [128, 1024], BF16))
        b[f"sq_bf{i}"] = ctx.enter_context(
            nc.sbuf_tensor(f"sq_bf{i}" + sfx, [128, 1024], BF16))
        b[f"nmean{i}"] = ctx.enter_context(
            nc.sbuf_tensor(f"nmean{i}" + sfx, [1, 128], F32))
        b[f"lnv{i}"] = ctx.enter_context(
            nc.sbuf_tensor(f"lnv{i}" + sfx, [1, 128], F32))
        b[f"rstd{i}"] = ctx.enter_context(
            nc.sbuf_tensor(f"rstd{i}" + sfx, [1, 128], F32))
    for i in range(2):
        b[f"scr_xn{i}"] = ctx.enter_context(
            nc.sbuf_tensor(f"scr_xn{i}" + sfx, [128, 1024], F32))


class LNEpi:

    def __init__(self, p, c, b, qc, proj_dc, ps_stat, xold, bias_cols=None,
                 xbf=None, xf32=None):
        self.p, self.c, self.b, self.qc = p, c, b, qc
        self.proj_dc, self.ps_stat, self.xold = proj_dc, ps_stat, xold
        self.bias_cols, self.xbf, self.xf32 = bias_cols, xbf, xf32
        self.par = (qc % 2) if b.get("_npar", 2) == 2 else 0
        self.xn_par = qc % 2
        self.vx = None

    def stage(self, k):
        p, c, b, qc = self.p, self.c, self.b, self.qc
        par, proj_dc, ps_stat = self.par, self.proj_dc, self.ps_stat
        xold, bias_cols = self.xold, self.bias_cols
        rb = b[f"rb_bf{par}"]
        sq = b[f"sq_bf{par}"]
        d_ = b[f"scr_d{par}"]
        nmean = b[f"nmean{par}"]
        lnv = b[f"lnv{par}"]
        rstd = b[f"rstd{par}"]
        xn = b[f"scr_xn{self.xn_par}"]
        if k == 0:
            if xold is None:
                self.vr = p.op("vector", lambda E: E.engine_nop(), inc="v")
                sr = None
                for dc in range(NJC):
                    fn = (lambda E, dc=dc: E.activation(
                        out=rb[:, dc * 128:(dc + 1) * 128], in_=proj_dc(dc),
                        func=AF.Identity))
                    r = p.op("scalar", fn, inc=("s" if dc == NJC - 1 else None))
                    sr = r if r is not None else sr
                self.sr = sr
            else:
                r_ = b[f"scr_r{par}"]
                vr = None
                for dc in range(NJC):
                    if bias_cols is None:
                        fn = (lambda E, dc=dc: E.tensor_add(
                            r_[:, dc * 128:(dc + 1) * 128], proj_dc(dc),
                            xold[:, dc * 128:(dc + 1) * 128]))
                    else:
                        fn = (lambda E, dc=dc: E.scalar_tensor_tensor(
                            out=r_[:, dc * 128:(dc + 1) * 128],
                            in0=proj_dc(dc),
                            scalar=bias_cols[:, dc:dc + 1],
                            in1=xold[:, dc * 128:(dc + 1) * 128],
                            op0=ALU.add, op1=ALU.add))
                    r = p.op("vector", fn, inc=("v" if dc == NJC - 1 else None))
                    vr = r if r is not None else vr
                self.vr = vr
                p.wait("scalar", "v", vr)
                self.sr = p.op("scalar", lambda E: E.activation(
                    out=rb[:], in_=r_[:], func=AF.Identity), inc="s")
        elif k == 1:
            p.wait("tensor", "s", self.sr)
            tm = None
            for jc in range(NJC):
                fn = (lambda E, jc=jc: E.matmul(
                    ps_stat[0:1, 0:128], c["ones128_bf"][:],
                    rb[:, jc * 128:(jc + 1) * 128],
                    start=(jc == 0), stop=(jc == NJC - 1)))
                r = p.op("tensor", fn, inc=("t" if jc == NJC - 1 else None))
                tm = r if r is not None else tm
            p.wait("vector", "t", tm)
            self.vm = p.op("vector", lambda E: E.tensor_scalar(
                out=nmean[:], in0=ps_stat[0:1, 0:128], scalar1=-1.0 / D,
                scalar2=None, op0=ALU.mult), inc="v")
        elif k == 2:
            p.wait("tensor", "v", self.vm)
            tb = p.op("tensor", lambda E: E.matmul(
                ps_stat[:, 128:256], c["ones_col_f32"][:], nmean[:],
                start=True, stop=True), inc="t")
            p.wait("vector", "t", tb)
            r_ = b.get(f"scr_r{par}")
            vd = None
            for jc in range(NJC):
                src_ap = (proj_dc(jc) if xold is None
                          else r_[:, jc * 128:(jc + 1) * 128])
                fn = (lambda E, jc=jc, src_ap=src_ap: E.tensor_add(
                    d_[:, jc * 128:(jc + 1) * 128], src_ap,
                    ps_stat[:, 128:256]))
                r = p.op("vector", fn, inc=("v" if jc == NJC - 1 else None))
                vd = r if r is not None else vd
            p.wait("scalar", "v", vd)
            self.sq = p.op("scalar", lambda E: E.activation(
                out=sq[:], in_=d_[:], func=AF.Square), inc="s")
        elif k == 3:
            p.wait("tensor", "s", self.sq)
            tv2 = None
            for jc in range(NJC):
                fn = (lambda E, jc=jc: E.matmul(
                    ps_stat[0:1, 256:384], c["ones128_bf"][:],
                    sq[:, jc * 128:(jc + 1) * 128],
                    start=(jc == 0), stop=(jc == NJC - 1)))
                r = p.op("tensor", fn, inc=("t" if jc == NJC - 1 else None))
                tv2 = r if r is not None else tv2
            p.wait("scalar", "t", tv2)
            p.op("scalar", lambda E: E.activation(
                out=lnv[:], in_=ps_stat[0:1, 256:384], func=AF.Ln,
                bias=c["eps_tile"][:], scale=1.0 / D))
            self.ss = p.op("scalar", lambda E: E.activation(
                out=rstd[:], in_=lnv[:], func=AF.Exp, scale=-0.5), inc="s")
        elif k == 4:
            p.wait("tensor", "s", self.ss)
            tb2 = p.op("tensor", lambda E: E.matmul(
                ps_stat[:, 384:512], c["ones_col_f32"][:], rstd[:],
                start=True, stop=True), inc="t")
            p.wait("vector", "t", tb2)
            vx = None
            for jc in range(NJC):
                fn = (lambda E, jc=jc: E.tensor_mul(
                    xn[:, jc * 128:(jc + 1) * 128],
                    d_[:, jc * 128:(jc + 1) * 128],
                    ps_stat[:, 384:512]))
                r = p.op("vector", fn, inc=("v" if jc == NJC - 1 else None))
                vx = r if r is not None else vx
            self.vx = vx
            xn3 = xn[:].rearrange("p (c t) -> p c t", c=NJC)
            if self.xbf is not None:
                p.wait("scalar", "v", vx)
                p.op("scalar", lambda E, xn3=xn3: E.activation(
                    out=self.xbf[:].rearrange(
                        "p (c t) -> p c t", c=NJC)[:, :, qc * 128:(qc + 1) * 128],
                    in_=xn3, func=AF.Identity), inc="s")
            if self.xf32 is not None:
                p.wait("gpsimd", "v", vx)
                p.op("gpsimd", lambda E, xn3=xn3: E.tensor_copy(
                    out=self.xf32[:].rearrange(
                        "p (c t) -> p c t", c=NJC)[:, :, qc * 128:(qc + 1) * 128],
                    in_=xn3), inc="g")

    def run_all(self):
        for k in range(5):
            self.stage(k)
        return self.vr, self.vx


def emit_ln_epilogue(p, c, b, qc, proj_dc, ps_stat, xold, bias_cols=None,
                    xbf=None, xf32=None):
    epi = LNEpi(p, c, b, qc, proj_dc, ps_stat, xold, bias_cols, xbf, xf32)
    return epi.run_all()


def emit_attention(p, c, ctx, layer, xt_in, xt_out, w_aps, rope=None,
                   bias_aps=None):
    nc = p.nc
    mask_bf = c["mask0_bf"] if layer == 0 else c["mask1_bf"]
    sfx = f"_a{layer}"

    b = {}
    xbf = ctx.enter_context(nc.sbuf_tensor("xbf" + sfx, [128, NJC * T], BF16))
    kt = ctx.enter_context(nc.sbuf_tensor("kt" + sfx, [128, NJC * T], BF16))
    vt = ctx.enter_context(nc.sbuf_tensor("vt" + sfx, [128, NTC * D], BF16))
    wqs = ctx.enter_context(nc.sbuf_tensor("wqs" + sfx, [128, NJC * 1024], BF16))
    wsh = ctx.enter_context(nc.sbuf_tensor("wsh" + sfx, [128, NJC * 1024], BF16))
    qbf = ctx.enter_context(nc.sbuf_tensor("qbf" + sfx, [128, NJC * 128], BF16))
    pbf = [ctx.enter_context(nc.sbuf_tensor(f"pbf{i}" + sfx, [128, T], BF16))
           for i in range(2)]
    ptsb = [ctx.enter_context(nc.sbuf_tensor(f"ptsb{i}" + sfx, [128, T], BF16))
            for i in range(3)]
    ybf = ctx.enter_context(nc.sbuf_tensor("ybf" + sfx, [128, NJC * 128], BF16))
    projsb = ctx.enter_context(nc.sbuf_tensor("projsb" + sfx, [128, 1024], F32))
    xold0 = ctx.enter_context(nc.sbuf_tensor("xold0" + sfx, [128, 1024], F32))
    xold1 = ctx.enter_context(nc.sbuf_tensor("xold1" + sfx, [128, 1024], F32))
    mrow = ctx.enter_context(nc.sbuf_tensor("mrow" + sfx, [128, 1], F32))
    kbacc = ctx.enter_context(nc.sbuf_tensor("kbacc" + sfx, [1, 1], F32))
    kbrow2 = ctx.enter_context(nc.sbuf_tensor("kbrow2" + sfx, [1, 2], F32))
    kbblk = ctx.enter_context(nc.sbuf_tensor("kbblk" + sfx, [128, 2], BF16))
    mneg = ctx.enter_context(nc.sbuf_tensor("mneg" + sfx, [128, 16], F32))
    mtmp = ctx.enter_context(nc.sbuf_tensor("mtmp" + sfx, [128, 16], F32))
    lrow = ctx.enter_context(nc.sbuf_tensor("lrow" + sfx, [128, 1], F32))
    rlrow = ctx.enter_context(nc.sbuf_tensor("rlrow" + sfx, [128, 1], F32))
    mrow2 = ctx.enter_context(nc.sbuf_tensor("mrow2" + sfx, [128, 1], F32))
    lrow2 = ctx.enter_context(nc.sbuf_tensor("lrow2" + sfx, [128, 1], F32))
    rlrow2 = ctx.enter_context(nc.sbuf_tensor("rlrow2" + sfx, [128, 1], F32))
    kscr1 = ctx.enter_context(nc.sbuf_tensor("kscr1" + sfx, [128, 1024], F32))
    ln_alloc(nc, ctx, b, sfx=sfx, n_par=1)
    if rope is not None:
        ropec = ctx.enter_context(nc.sbuf_tensor("ropec_sb", [128, T], BF16))
        ropes = ctx.enter_context(nc.sbuf_tensor("ropes_sb", [128, T], BF16))
    if bias_aps is not None:
        bqf = ctx.enter_context(nc.sbuf_tensor("bqf_sb", [128, 8], F32))
        bkf = ctx.enter_context(nc.sbuf_tensor("bkf_sb", [128, 8], F32))
        bof = ctx.enter_context(nc.sbuf_tensor("bof_sb", [128, 8], F32))
        bvrow = ctx.enter_context(nc.sbuf_tensor("bvrow_sb", [1, 1024], BF16))

    ps_s = ctx.enter_context(nc.psum_tensor("ps_s" + sfx, [128, 2048], F32))
    ps_tq = ctx.enter_context(nc.psum_tensor("ps_tq" + sfx, [128, 1024], F32))
    ps_t = ps_tq[:].bitcast(BF16)
    ps_w = ctx.enter_context(nc.psum_tensor("ps_w" + sfx, [128, 512], F32))
    ps_x = ctx.enter_context(nc.psum_tensor("ps_x" + sfx, [128, 512], F32))

    xre = xt_in.rearrange("(c p) t -> p c t", p=128)
    xbf3 = xbf[:].rearrange("p (c t) -> p c t", c=NJC)
    p.op("gpsimd", lambda E: E.dma_start(out=xbf3, in_=xre),
         inc="w", amount=16)
    if layer == 0:
        wsrc = w_aps["wqkv"].rearrange("(c p) o -> p c o", p=128)
        wqs3 = wqs[:].rearrange("p (c o) -> p c o", c=NJC)
        wsh3 = wsh[:].rearrange("p (c o) -> p c o", c=NJC)
        p.op("gpsimd", lambda E: E.dma_start(out=wqs3, in_=wsrc[:, :, 0:1024]),
             inc="w", amount=16)
        p.op("gpsimd", lambda E: E.dma_start(out=wsh3, in_=wsrc[:, :, 1024:2048]),
             inc="w", amount=16)
    else:
        wqs3 = wqs[:].rearrange("p (c o) -> p c o", c=NJC)
        wsh3 = wsh[:].rearrange("p (c o) -> p c o", c=NJC)
        src = w_aps["wqT"].rearrange("(c p) o -> p c o", p=128)
        p.op("gpsimd", lambda E, src=src: E.dma_start(out=wqs3, in_=src),
             inc="w", amount=16)
        src = w_aps["wkT"].rearrange("(c p) o -> p c o", p=128)
        p.op("gpsimd", lambda E, src=src: E.dma_start(out=wsh3, in_=src),
             inc="w", amount=16)
        p.op("gpsimd", lambda E: E.dma_start(out=bqf[:], in_=bias_aps["bq"]),
             inc="w", amount=16)
        p.op("gpsimd", lambda E: E.dma_start(out=bkf[:], in_=bias_aps["bk"]),
             inc="w", amount=16)
        p.op("gpsimd", lambda E: E.dma_start(out=bof[:], in_=bias_aps["bo"]),
             inc="w", amount=16)
        p.op("gpsimd", lambda E: E.dma_start(out=bvrow[:], in_=bias_aps["bv"]),
             inc="w", amount=16)
    w0 = p.cnt["w"]
    if rope is not None:
        p.op("gpsimd", lambda E: E.dma_start(out=ropec[:], in_=rope["cos"]),
             inc="ld2", amount=16)
        p.op("gpsimd", lambda E: E.dma_start(out=ropes[:], in_=rope["sin"]),
             inc="ld2", amount=16)
        p.wait("vector", "ld2", p.cnt["ld2"])

    p.wait("vector", "w", w0)
    vscale = p.op("vector", lambda E: E.tensor_scalar(
        out=wqs[:], in0=wqs[:], scalar1=0.125, scalar2=None, op0=ALU.mult),
        inc="v")
    if bias_aps is not None:
        vscale = p.op("vector", lambda E: E.tensor_scalar(
            out=bqf[:], in0=bqf[:], scalar1=0.125, scalar2=None, op0=ALU.mult),
            inc="v")

    p.wait("tensor", "w", w0)
    p.wait("tensor", "g", p.cnt["g"])
    NTT = 4
    kv_cons = {0: 0, 1: 0}
    kv_sem = "v" if rope is not None else "s"
    for ic in range(NJC):
        for tt in range(NTT):
            pid = (ic * NTT + tt) % 2
            ps = ps_w if pid == 0 else ps_x
            p.wait("tensor", kv_sem, kv_cons[pid])
            tk = None
            for jc in range(NJC):
                fn = (lambda E, jc=jc, ic=ic, tt=tt, ps=ps: E.matmul(
                    ps[:], wsh[:, jc * 1024 + ic * 128:jc * 1024 + (ic + 1) * 128],
                    xbf[:, jc * T + tt * 512:jc * T + tt * 512 + 512],
                    start=(jc == 0), stop=(jc == NJC - 1)))
                r = p.op("tensor", fn, inc=("t" if jc == NJC - 1 else None))
                tk = r if r is not None else tk
            cs, ce = ic * T + tt * 512, ic * T + tt * 512 + 512
            ts_, te_ = tt * 512, tt * 512 + 512
            if rope is not None:
                p.wait("vector", "t", tk)
                p.op("vector", lambda E, ps=ps, ts_=ts_, te_=te_: E.tensor_mul(
                    kscr1[:, 512:1024], ps[:], ropec[:, ts_:te_]))
                for olo, ilo in ((0, 32), (32, 0), (64, 96), (96, 64)):
                    fn = (lambda E, ps=ps, ts_=ts_, te_=te_, olo=olo, ilo=ilo:
                          E.tensor_mul(kscr1[olo:olo + 32, 0:512],
                                       ps[ilo:ilo + 32, :],
                                       ropes[olo:olo + 32, ts_:te_]))
                    p.op("vector", fn)
                kv_cons[pid] = p.op("vector", lambda E, cs=cs, ce=ce: E.tensor_add(
                    kt[:, cs:ce], kscr1[:, 512:1024], kscr1[:, 0:512]), inc="v")
            else:
                p.wait("scalar", "t", tk)
                kv_cons[pid] = p.op(
                    "scalar", lambda E, ps=ps, cs=cs, ce=ce, ic=ic: E.activation(
                        out=kt[:, cs:ce], in_=ps[:], func=AF.Identity,
                        bias=bkf[:, ic:ic + 1]), inc="s")
    k_done_t = p.cnt["t"]
    k_done_kv = p.cnt["v" if rope is not None else "s"]

    ksq = kscr1[:].bitcast(BF16)
    p.wait("scalar", "v" if rope is not None else "s", k_done_kv)
    p.op("vector", lambda E: E.memset(kbacc[:], 0.0))
    tn = 0
    kn_v = 0
    for ic in range(NJC):
        p.wait("scalar", "t", tn)
        sqs = p.op("scalar", lambda E, ic=ic: E.activation(
            out=ksq[:, 0:2048], in_=kt[:, ic * T:(ic + 1) * T],
            func=AF.Square), inc="s")
        p.wait("tensor", "s", sqs)
        p.wait("tensor", "v", kn_v if ic > 0 else 0)
        tn = None
        for sb4 in range(4):
            fn = (lambda E, sb4=sb4: E.matmul(
                ps_s[0:1, sb4 * 512:sb4 * 512 + 512], c["ones128_bf"][:],
                ksq[:, sb4 * 512:(sb4 + 1) * 512], start=True, stop=True))
            r = p.op("tensor", fn, inc=("t" if sb4 == 3 else None))
            tn = r if r is not None else tn
        p.wait("vector", "t", tn)
        p.op("vector", lambda E: E.reduce_max(
            mrow[0:1, 0:1], ps_s[0:1, 0:2048], axis=AX.X))
        kn_v = p.op("vector", lambda E: E.tensor_max(
            kbacc[:], kbacc[:], mrow[0:1, 0:1]), inc="v")
    p.op("vector", lambda E: E.tensor_scalar(
        out=kbrow2[0:1, 0:1], in0=kbacc[:], scalar1=1.08, scalar2=None,
        op0=ALU.mult))
    kn_v = p.op("vector", lambda E: E.tensor_scalar(
        out=kbrow2[0:1, 1:2], in0=kbacc[:], scalar1=1.08, scalar2=None,
        op0=ALU.mult), inc="v")
    p.wait("tensor", "v", kn_v)
    tn = p.op("tensor", lambda E: E.matmul(
        ps_s[:, 0:2], c["ones_col_f32"][:], kbrow2[:], start=True, stop=True),
        inc="t")
    p.wait("vector", "t", tn)
    kblk_v = p.op("vector", lambda E: E.tensor_mul(
        kbblk[:], ps_s[:, 0:2], c["ones2blk"][:]), inc="v")

    p.wait("gpsimd", "t", k_done_t)
    if layer == 0:
        wsrc = w_aps["wqkv"].rearrange("(c p) o -> p c o", p=128)
        p.op("gpsimd", lambda E: E.dma_start(out=wsh3, in_=wsrc[:, :, 2048:3072]),
             inc="w", amount=16)
    else:
        src = w_aps["wvT"].rearrange("(c p) o -> p c o", p=128)
        p.op("gpsimd", lambda E, src=src: E.dma_start(out=wsh3, in_=src),
             inc="w", amount=16)
    w1 = p.cnt["w"]

    p.wait("tensor", "w", w1)
    v_cons = {0: kv_cons[0], 1: kv_cons[1]} if rope is None else {0: 0, 1: 0}
    if rope is not None:
        p.wait("tensor", "v", p.cnt["v"])
    for tc in range(NTC):
        for ih in range(2):
            pid = (tc * 2 + ih) % 2
            ps = ps_w if pid == 0 else ps_x
            p.wait("tensor", "s", v_cons[pid])
            tv = None
            for jc in range(NJC):
                last = (jc == NJC - 1 and bias_aps is None)
                fn = (lambda E, jc=jc, tc=tc, ih=ih, ps=ps, last=last: E.matmul(
                    ps[:], xbf[:, jc * T + tc * 128:jc * T + (tc + 1) * 128],
                    wsh[:, jc * 1024 + ih * 512:jc * 1024 + ih * 512 + 512],
                    start=(jc == 0), stop=last))
                r = p.op("tensor", fn, inc=("t" if last else None))
                tv = r if r is not None else tv
            if bias_aps is not None:
                fn = (lambda E, ih=ih, ps=ps: E.matmul(
                    ps[:], c["ones_col_bf"][:],
                    bvrow[0:1, ih * 512:ih * 512 + 512], start=False, stop=True))
                tv = p.op("tensor", fn, inc="t")
            p.wait("scalar", "t", tv)
            fn = (lambda E, ps=ps, tc=tc, ih=ih: E.activation(
                out=vt[:, tc * D + ih * 512:tc * D + ih * 512 + 512],
                in_=ps[:], func=AF.Identity))
            v_cons[pid] = p.op("scalar", fn, inc="s")
    v_done_t = p.cnt["t"]

    p.wait("gpsimd", "t", v_done_t)
    if layer == 0:
        src = w_aps["wproj"].rearrange("(c p) o -> p c o", p=128)
    else:
        src = w_aps["woT"].rearrange("(c p) o -> p c o", p=128)
    p.op("gpsimd", lambda E, src=src: E.dma_start(out=wsh3, in_=src),
         inc="w", amount=16)
    w2 = p.cnt["w"]
    p.wait("tensor", "w", w2)
    p.wait("tensor", "v", vscale)

    xre_old = xt_in.rearrange("(c p) t -> p c t", p=128)
    xout_re = xt_out.rearrange("(c p) t -> p c t", p=128)

    pend = None
    pend_store = None
    vr_hist = []
    st_hist = []
    prev_ve_last = 0
    prev_evproj = kblk_v
    prev_se = 0
    for qc in range(NTC):
        xold = xold0 if qc % 2 == 0 else xold1
        ldsem = "ld" if qc % 2 == 0 else "lda"
        if qc >= 2:
            p.wait("sync", "v", vr_hist[qc - 2])
        fn = (lambda E, qc=qc, xold=xold: E.dma_start(
            out=xold[:].rearrange("p (c t) -> p c t", c=NJC),
            in_=xre_old[:, :, qc * 128:(qc + 1) * 128]))
        ld_x = p.op("sync", fn, inc=ldsem, amount=16)

        p.wait("tensor", "v", prev_ve_last)
        tq = None
        for ic in range(NJC):
            psl = ps_tq[:, ic * 128:(ic + 1) * 128]
            for jc in range(NJC):
                fn = (lambda E, jc=jc, ic=ic, qc=qc, psl=psl: E.matmul(
                    psl, wqs[:, jc * 1024 + ic * 128:jc * 1024 + (ic + 1) * 128],
                    xbf[:, jc * T + qc * 128:jc * T + (qc + 1) * 128],
                    start=(jc == 0), stop=(jc == NJC - 1)))
                r = p.op("tensor", fn,
                         inc=("t" if ic == NJC - 1 and jc == NJC - 1 else None))
                tq = r if r is not None else tq
        if rope is not None:
            ts0 = qc * 128
            p.wait("vector", "t", tq)
            rc = ropec[:, ts0:ts0 + 128].rearrange(
                "p (c t) -> p c t", c=1).broadcast_to([128, NJC, 128])
            rs = [ropes[olo:olo + 32, ts0:ts0 + 128].rearrange(
                "p (c t) -> p c t", c=1).broadcast_to([32, NJC, 128])
                  for olo in (0, 32, 64, 96)]
            qsc3 = ps_tq[:].rearrange("p (c t) -> p c t", c=NJC)
            for gi, (olo, ilo) in enumerate(((0, 32), (32, 0), (64, 96), (96, 64))):
                fn = (lambda E, olo=olo, ilo=ilo, rs=rs, gi=gi, qsc3=qsc3:
                      E.tensor_tensor(
                          out=kscr1[olo:olo + 32, :].rearrange(
                              "p (c t) -> p c t", c=NJC),
                          in0=qsc3[ilo:ilo + 32, :, :], in1=rs[gi], op=ALU.mult))
                p.op("vector", fn)
            p.op("vector", lambda E, rc=rc, qsc3=qsc3: E.tensor_tensor(
                out=qsc3, in0=qsc3, in1=rc, op=ALU.mult))
            q_rope_v = p.op("vector", lambda E: E.tensor_add(
                qbf[:], ps_tq[:], kscr1[:]), inc="v")
            p.wait("tensor", "v", q_rope_v)
            qrdy_s, qrdy_v = None, q_rope_v
        else:
            p.wait("scalar", "t", tq)
            sq_ = None
            for ic in range(NJC):
                psl = ps_tq[:, ic * 128:(ic + 1) * 128]
                qs, qe = ic * 128, (ic + 1) * 128
                r = p.op(
                    "scalar", lambda E, psl=psl, qs=qs, qe=qe, ic=ic: E.activation(
                        out=qbf[:, qs:qe], in_=psl, func=AF.Identity,
                        bias=bqf[:, ic:ic + 1]), inc=("s" if ic == NJC - 1 else None))
                sq_ = r if r is not None else sq_
            p.wait("tensor", "s", sq_)
            qrdy_s, qrdy_v = sq_, None

        qsq = kscr1[:].bitcast(BF16)
        if qrdy_s is not None:
            p.wait("scalar", "s", qrdy_s)
        else:
            p.wait("scalar", "v", qrdy_v)
        sqq = p.op("scalar", lambda E: E.activation(
            out=qsq[:, 0:1024], in_=qbf[:], func=AF.Square), inc="s")
        p.wait("tensor", "s", sqq)
        p.wait("tensor", "v", kblk_v)
        tqn = None
        for ic in range(NJC):
            fn = (lambda E, ic=ic: E.matmul(
                ps_w[:, ic * 2:ic * 2 + 2],
                qsq[:, ic * 128:(ic + 1) * 128], kbblk[:],
                start=True, stop=True))
            r = p.op("tensor", fn, inc=("t" if ic == NJC - 1 else None))
            tqn = r if r is not None else tqn
        p.wait("scalar", "t", tqn)
        p.op("scalar", lambda E: E.activation(
            out=mtmp[:], in_=ps_w[:, 0:16], func=AF.Ln))
        smh = p.op("scalar", lambda E: E.activation(
            out=mtmp[:], in_=mtmp[:], func=AF.Exp, scale=0.5), inc="s")
        p.wait("vector", "s", smh)
        vmh = p.op("vector", lambda E: E.tensor_scalar(
            out=mneg[:], in0=mtmp[:], scalar1=-1.0, scalar2=-1.0,
            op0=ALU.mult, op1=ALU.add), inc="v")
        p.wait("scalar", "v", vmh)

        ncols = (qc + 1) * 128
        se_h = [0] * NH
        vp_h = [0] * NH
        ve_h = [0] * (NH + 4)
        tS_h = [0] * NH
        tT_h = [0] * (NH + 4)
        tPV_h = [0] * (NH + 4)
        yev_h = [0] * (NH + 4)
        epi_hooks = {2: 1, 5: 2, 8: 3, 11: 4}
        for h in range(NH + 2):
            if h < NH:
                half = (h % 2) * 64
                ic = h // 2
                dual = qc <= 7
                sb = (h % 2) * 1024 if dual else 0
                if h == 0:
                    p.wait("tensor", "v", prev_evproj)
                    p.wait("tensor", "s", prev_se)
                elif dual and h >= 2:
                    p.wait("tensor", "s", se_h[h - 2])
                elif not dual:
                    p.wait("tensor", "s", se_h[h - 1])
                for kb0 in range(0, qc + 1, 4):
                    nn = min(4, qc + 1 - kb0) * 128
                    fn = (lambda E, kb0=kb0, nn=nn, half=half, ic=ic, qc=qc, sb=sb:
                          E.matmul(
                        ps_s[:, sb + kb0 * 128:sb + kb0 * 128 + nn],
                        qbf[half:half + 64, ic * 128:(ic + 1) * 128],
                        kt[half:half + 64, ic * T + kb0 * 128:ic * T + kb0 * 128 + nn],
                        start=True, stop=(kb0 + 4 <= qc)))
                    p.op("tensor", fn)
                fn = (lambda E, qc=qc, sb=sb: E.matmul(
                    ps_s[:, sb + qc * 128:sb + qc * 128 + 128], c["ident_bf"][:],
                    mask_bf[:], start=False, stop=True))
                tS_h[h] = p.op("tensor", fn, inc="t")
                lr_ = lrow if h % 2 == 0 else lrow2
                rl_ = rlrow if h % 2 == 0 else rlrow2
                p.wait("scalar", "t", tS_h[h])
                se_h[h] = p.op(
                    "scalar", lambda E, ncols=ncols, h=h, sb=sb, lr_=lr_:
                    E.activation(
                        out=pbf[h % 2][:, 0:ncols], in_=ps_s[:, sb:sb + ncols],
                        func=AF.Exp, bias=mneg[:, h:h + 1], accum_out=lr_[:]),
                    inc="s")
                p.wait("vector", "s", se_h[h])
                p.drain("vector")
                p.op("vector", lambda E, rl_=rl_, lr_=lr_: E.reciprocal(
                    rl_[:], lr_[:]))
                p.drain("vector")
                vp_h[h] = p.op("vector", lambda E, ncols=ncols, h=h, rl_=rl_:
                               E.tensor_scalar(
                    out=pbf[h % 2][:, 0:ncols], in0=pbf[h % 2][:, 0:ncols],
                    scalar1=rl_[:], scalar2=None, op0=ALU.mult), inc="v")
            if 1 <= h <= NH:
                hh = h - 1
                p.wait("tensor", "v", vp_h[hh])
                if hh >= 1:
                    p.wait("tensor", "v", ve_h[hh - 1])
                tt_ = None
                for kb in range(qc + 1):
                    fn = (lambda E, kb=kb, hh=hh: E.transpose(
                        ps_t[:, kb * 128:(kb + 1) * 128],
                        pbf[hh % 2][:, kb * 128:(kb + 1) * 128], c["ident_bf"][:]))
                    r = p.op("tensor", fn, inc=("t" if kb == qc else None))
                    tt_ = r if r is not None else tt_
                tT_h[hh] = tt_
                p.wait("vector", "t", tt_)
                if hh >= 3:
                    p.wait("vector", "t", tPV_h[hh - 3])
                ve_h[hh] = p.op("vector", lambda E, ncols=ncols, hh=hh: E.tensor_copy(
                    out=ptsb[hh % 3][:, 0:ncols], in_=ps_t[:, 0:ncols]), inc="v")
            if 2 <= h <= NH + 1:
                hh = h - 2
                half2 = (hh % 2) * 64
                ic2 = hh // 2
                p.wait("tensor", "v", ve_h[hh])
                if hh >= 2 and hh % 2 == 0:
                    p.wait("tensor", "v", yev_h[hh - 1])
                tp_kw = {} if hh % 2 == 0 else {"tile_position": (0, 64)}
                tpv = None
                for kb in range(qc + 1):
                    fn = (lambda E, kb=kb, half2=half2, hh=hh, qc=qc, tp_kw=tp_kw:
                          E.matmul(
                        ps_w[half2:half2 + 64, 0:128],
                        vt[:, kb * D + hh * HD:kb * D + (hh + 1) * HD],
                        ptsb[hh % 3][:, kb * 128:(kb + 1) * 128],
                        start=(kb == 0), stop=(kb == qc), **tp_kw))
                    r = p.op("tensor", fn, inc=("t" if kb == qc else None))
                    tpv = r if r is not None else tpv
                tPV_h[hh] = tpv
                if hh % 2 == 1:
                    p.wait("vector", "t", tpv)
                    yev_h[hh] = p.op("vector", lambda E, ic2=ic2: E.tensor_copy(
                        out=ybf[:, ic2 * 128:(ic2 + 1) * 128], in_=ps_w[:, 0:128]),
                        inc="v")
                    yev_h[hh + 1] = yev_h[hh]
            if pend is not None and h in epi_hooks:
                pend.stage(epi_hooks[h])
                if epi_hooks[h] == 4:
                    pend_store()
                    pend = None

        p.wait("tensor", "v", p.cnt["v"])
        p.wait("tensor", "s", se_h[NH - 1])
        tpj = None
        for dc in range(NJC):
            for jc in range(NJC):
                fn = (lambda E, dc=dc, jc=jc: E.matmul(
                    ps_s[:, dc * 128:(dc + 1) * 128],
                    wsh[:, jc * 1024 + dc * 128:jc * 1024 + (dc + 1) * 128],
                    ybf[:, jc * 128:(jc + 1) * 128],
                    start=(jc == 0), stop=(jc == NJC - 1)))
                r = p.op("tensor", fn,
                         inc=("t" if jc == NJC - 1 and dc == NJC - 1 else None))
                tpj = r if r is not None else tpj
        p.wait("vector", "t", tpj)
        prev_evproj = p.op("vector", lambda E: E.tensor_copy(
            out=projsb[:], in_=ps_s[:, 0:1024]), inc="v")
        prev_se = p.cnt["s"]
        prev_ve_last = p.cnt["v"]

        p.wait("vector", ldsem, ld_x)
        if qc >= 2:
            p.wait("vector", "st", st_hist[-1])
        epi = LNEpi(p, c, b, qc,
                    lambda dc: projsb[:, dc * 128:(dc + 1) * 128], ps_x,
                    xold, bias_cols=(bof if bias_aps is not None else None),
                    xbf=xbf)
        epi.stage(0)
        vr_hist.append(epi.vr)

        def mk_store(qc, epi):
            def store():
                p.wait("sync", "v", epi.vx)
                xn = b[f"scr_xn{qc % 2}"]
                fn = (lambda E, qc=qc, xn=xn: E.dma_start(
                    out=xout_re[:, :, qc * 128:(qc + 1) * 128],
                    in_=xn[:].rearrange("p (c t) -> p c t", c=NJC)))
                p.op("sync", fn, inc="st", amount=16)
                st_hist.append(p.cnt["st"])
            return store
        pend = epi
        pend_store = mk_store(qc, epi)
    for k in (1, 2, 3, 4):
        pend.stage(k)
    pend_store()
    return b


def emit_l0_ffn(p, c, ctx, xt_in, xt_out, w):
    nc = p.nc
    b = {}
    ictx = ExitStack()
    outf = ctx.enter_context(nc.sbuf_tensor("ffn_outf", [128, NJC * T], F32))
    outbf = ictx.enter_context(nc.sbuf_tensor("ffn_outbf", [128, NJC * T], BF16))
    cnw1s = ictx.enter_context(nc.sbuf_tensor("cnw1s", [128, NJC * 1024], BF16))
    cnw2s = ictx.enter_context(nc.sbuf_tensor("cnw2s", [128, NJC * 512], BF16))
    cnw3s = ictx.enter_context(nc.sbuf_tensor("cnw3s", [128, 4 * 256], BF16))
    cnw4s = ictx.enter_context(nc.sbuf_tensor("cnw4s", [128, 2], BF16))
    gcw1s = ictx.enter_context(nc.sbuf_tensor("gcw1s", [128, NJC * 512], BF16))
    gcw2s = ictx.enter_context(nc.sbuf_tensor("gcw2s", [128, 4 * 256], BF16))
    gcw3s = ictx.enter_context(nc.sbuf_tensor("gcw3s", [128, 2], BF16))
    cb1 = ictx.enter_context(nc.sbuf_tensor("cb1s", [128, 8], F32))
    cb2 = ictx.enter_context(nc.sbuf_tensor("cb2s", [128, 4], F32))
    cb3 = ictx.enter_context(nc.sbuf_tensor("cb3s", [128, 2], F32))
    cb4 = ictx.enter_context(nc.sbuf_tensor("cb4s", [1, 1], F32))
    gb1 = ictx.enter_context(nc.sbuf_tensor("gb1s", [128, 4], F32))
    gb2 = ictx.enter_context(nc.sbuf_tensor("gb2s", [128, 2], F32))
    gb3 = ictx.enter_context(nc.sbuf_tensor("gb3s", [1, 1], F32))
    arv = ictx.enter_context(nc.sbuf_tensor("arvs", [1, 1], F32))
    h1bf = ictx.enter_context(nc.sbuf_tensor("h1bf", [128, NJC * 512], BF16))
    h2bf = ictx.enter_context(nc.sbuf_tensor("h2bf", [128, 4 * 512], BF16))
    h3bf = ictx.enter_context(nc.sbuf_tensor("h3bf", [128, 2 * 512], BF16))
    gsum = ictx.enter_context(nc.sbuf_tensor("gsum", [128, 8], F32))
    gvec_bf = ictx.enter_context(nc.sbuf_tensor("gvec_bf", [128, 8], BF16))
    gh1_bf = ictx.enter_context(nc.sbuf_tensor("gh1_bf", [128, 4], BF16))
    gh2_bf = ictx.enter_context(nc.sbuf_tensor("gh2_bf", [128, 2], BF16))
    gfv = ictx.enter_context(nc.sbuf_tensor("gfv", [1, 1], F32))
    t11a = ictx.enter_context(nc.sbuf_tensor("t11a", [1, 1], F32))
    t11b = ictx.enter_context(nc.sbuf_tensor("t11b", [1, 1], F32))
    acoef = ictx.enter_context(nc.sbuf_tensor("acoef", [1, 1], F32))
    bcoef = ictx.enter_context(nc.sbuf_tensor("bcoef", [1, 1], F32))
    tfrow = ictx.enter_context(nc.sbuf_tensor("tfrow", [1, T], F32))
    crow = tfrow
    cbsb = ictx.enter_context(nc.sbuf_tensor("cbsb", [128, T], F32))
    twu = ictx.enter_context(nc.sbuf_tensor("twu", [128, 512], F32))
    twu2 = ictx.enter_context(nc.sbuf_tensor("twu2", [128, 512], F32))
    ps_w = ctx.enter_context(nc.psum_tensor("ps_w_f", [128, 512], F32))
    ps_x = ctx.enter_context(nc.psum_tensor("ps_x_f", [128, 512], F32))
    ps_gf = ctx.enter_context(nc.psum_tensor("ps_gf_f", [128, 16], F32))

    xre = xt_in.rearrange("(c p) t -> p c t", p=128)
    outf3 = outf[:].rearrange("p (c t) -> p c t", c=NJC)
    outbf3 = outbf[:].rearrange("p (c t) -> p c t", c=NJC)
    p.op("sync", lambda E: E.dma_start(out=outf3, in_=xre), inc="ld2", amount=16)
    p.op("gpsimd", lambda E: E.dma_start(out=outbf3, in_=xre), inc="w", amount=16)
    for name, dst, nchunks in (("cnw1", cnw1s, 8), ("cnw2", cnw2s, 8),
                               ("cnw3", cnw3s, 4), ("cnw4", cnw4s, 2),
                               ("gcw1", gcw1s, 8), ("gcw2", gcw2s, 4),
                               ("gcw3", gcw3s, 2)):
        src = w[name].rearrange("(c p) o -> p c o", p=128)
        dst3 = dst[:].rearrange("p (c o) -> p c o", c=nchunks)
        slow = name in ("cnw4", "gcw3")
        p.op("gpsimd", lambda E, src=src, dst3=dst3, slow=slow: E.dma_start(
            out=dst3, in_=src, allow_slow_non_contiguous=slow),
            inc="w", amount=16)
    for name, dst in (("cb1", cb1), ("cb2", cb2), ("cb3", cb3), ("cb4", cb4),
                      ("gb1", gb1), ("gb2", gb2), ("gb3", gb3), ("ar", arv)):
        p.op("sync", lambda E, name=name, dst=dst: E.dma_start(
            out=dst[:], in_=w[name]), inc="ld2", amount=16)
    w0 = p.cnt["w"]
    ld0 = p.cnt["ld2"]

    def emit_gf_chain():
        p.wait("vector", "ld2", ld0)
        p.wait("vector", "w", w0)
        p.wait("tensor", "w", w0)
        p.wait("tensor", "g", p.cnt["g"])
        p.wait("scalar", "ld2", ld0)
        for jc in range(NJC):
            p.op("vector", lambda E, jc=jc: E.reduce_sum(
                gsum[:, jc:jc + 1], outf[:, jc * T:(jc + 1) * T], axis=AX.X))
        vg = p.op("vector", lambda E: E.tensor_scalar(
            out=gvec_bf[:], in0=gsum[:], scalar1=1.0 / T, scalar2=None,
            op0=ALU.mult), inc="v")
        p.wait("tensor", "v", vg)
        tg = None
        for fc in range(4):
            for jc in range(NJC):
                fn = (lambda E, fc=fc, jc=jc: E.matmul(
                    ps_gf[:, fc:fc + 1],
                    gcw1s[:, jc * 512 + fc * 128:jc * 512 + (fc + 1) * 128],
                    gvec_bf[:, jc:jc + 1],
                    start=(jc == 0), stop=(jc == NJC - 1)))
                r = p.op("tensor", fn,
                         inc=("t" if fc == 3 and jc == NJC - 1 else None))
                tg = r if r is not None else tg
        p.wait("scalar", "t", tg)
        sg = None
        for fc in range(4):
            r = p.op("scalar", lambda E, fc=fc: E.activation(
                out=gh1_bf[:, fc:fc + 1], in_=ps_gf[:, fc:fc + 1], func=AF.Gelu,
                bias=gb1[:, fc:fc + 1]), inc=("s" if fc == 3 else None))
            sg = r if r is not None else sg
        p.wait("tensor", "s", sg)
        tg = None
        for fc2 in range(2):
            for fc1 in range(4):
                fn = (lambda E, fc2=fc2, fc1=fc1: E.matmul(
                    ps_gf[:, 8 + fc2:8 + fc2 + 1],
                    gcw2s[:, fc1 * 256 + fc2 * 128:fc1 * 256 + (fc2 + 1) * 128],
                    gh1_bf[:, fc1:fc1 + 1],
                    start=(fc1 == 0), stop=(fc1 == 3)))
                r = p.op("tensor", fn,
                         inc=("t" if fc2 == 1 and fc1 == 3 else None))
                tg = r if r is not None else tg
        p.wait("scalar", "t", tg)
        sg = None
        for fc in range(2):
            r = p.op("scalar", lambda E, fc=fc: E.activation(
                out=gh2_bf[:, fc:fc + 1], in_=ps_gf[:, 8 + fc:8 + fc + 1], func=AF.Gelu,
                bias=gb2[:, fc:fc + 1]), inc=("s" if fc == 1 else None))
            sg = r if r is not None else sg
        p.wait("tensor", "s", sg)
        tg = None
        for fc in range(2):
            fn = (lambda E, fc=fc: E.matmul(
                ps_gf[0:1, 12:13], gcw3s[:, fc:fc + 1], gh2_bf[:, fc:fc + 1],
                start=(fc == 0), stop=(fc == 1)))
            r = p.op("tensor", fn, inc=("t" if fc == 1 else None))
            tg = r if r is not None else tg
        p.wait("scalar", "t", tg)
        sgf = p.op("scalar", lambda E: E.activation(
            out=gfv[:], in_=ps_gf[0:1, 12:13], func=AF.Sigmoid, bias=gb3[:]), inc="s")
        p.wait("vector", "s", sgf)
        p.op("vector", lambda E: E.tensor_scalar(
            out=t11a[:], in0=gfv[:], scalar1=0.7, scalar2=-0.5, op0=ALU.mult,
            op1=ALU.add))
        p.op("vector", lambda E: E.tensor_mul(t11b[:], t11a[:], arv[:]))
        p.op("vector", lambda E: E.tensor_scalar(
            out=acoef[:], in0=t11b[:], scalar1=2.0, scalar2=0.1, op0=ALU.mult,
            op1=ALU.add))
        vab = p.op("vector", lambda E: E.tensor_scalar(
            out=bcoef[:], in0=arv[:], scalar1=0.6, scalar2=None, op0=ALU.mult),
            inc="v")


    p.wait("tensor", "w", w0)
    p.wait("tensor", "g", p.cnt["g"])
    p.wait("scalar", "ld2", ld0)
    PAIRS = ((0, 4, ALU.add, ALU.subtract), (1, 5, ALU.add, ALU.subtract),
             (2, 6, ALU.subtract, ALU.add), (3, 7, ALU.subtract, ALU.add))
    fbank = {0: 0, 1: 0}
    refresh_s = 0
    gf_emitted = False
    for cyc in range(3):
        for tt in range(4):
            tcs, tce = tt * 512, tt * 512 + 512
            for fc in range(NJC):
                pid = fc % 2
                ps = ps_w if pid == 0 else ps_x
                p.wait("tensor", "s", max(fbank[pid], refresh_s))
                th = None
                for jc in range(NJC):
                    fn = (lambda E, fc=fc, jc=jc, tcs=tcs, ps=ps: E.matmul(
                        ps[:],
                        cnw1s[:, jc * 1024 + fc * 128:jc * 1024 + (fc + 1) * 128],
                        outbf[:, jc * T + tcs:jc * T + tcs + 512],
                        start=(jc == 0), stop=(jc == NJC - 1)))
                    r = p.op("tensor", fn, inc=("t" if jc == NJC - 1 else None))
                    th = r if r is not None else th
                p.wait("scalar", "t", th)
                fbank[pid] = p.op(
                    "scalar", lambda E, ps=ps, fc=fc: E.activation(
                        out=h1bf[:, fc * 512:(fc + 1) * 512],
                        in_=ps[:], func=AF.Gelu, bias=cb1[:, fc:fc + 1]), inc="s")
                if not gf_emitted:
                    emit_gf_chain()
                    gf_emitted = True
            h1_done = p.cnt["s"]
            for fc2 in range(4):
                pid = fc2 % 2
                ps = ps_w if pid == 0 else ps_x
                p.wait("tensor", "s", max(fbank[pid], h1_done))
                th = None
                for fc1 in range(NJC):
                    fn = (lambda E, fc2=fc2, fc1=fc1, ps=ps: E.matmul(
                        ps[:],
                        cnw2s[:, fc1 * 512 + fc2 * 128:fc1 * 512 + (fc2 + 1) * 128],
                        h1bf[:, fc1 * 512:(fc1 + 1) * 512],
                        start=(fc1 == 0), stop=(fc1 == NJC - 1)))
                    r = p.op("tensor", fn, inc=("t" if fc1 == NJC - 1 else None))
                    th = r if r is not None else th
                p.wait("scalar", "t", th)
                fbank[pid] = p.op(
                    "scalar", lambda E, ps=ps, fc2=fc2: E.activation(
                        out=h2bf[:, fc2 * 512:(fc2 + 1) * 512], in_=ps[:],
                        func=AF.Gelu, bias=cb2[:, fc2:fc2 + 1]), inc="s")
            h2_done = p.cnt["s"]
            for fc3 in range(2):
                pid = fc3 % 2
                ps = ps_w if pid == 0 else ps_x
                p.wait("tensor", "s", max(fbank[pid], h2_done))
                th = None
                for fc2 in range(4):
                    fn = (lambda E, fc3=fc3, fc2=fc2, ps=ps: E.matmul(
                        ps[:],
                        cnw3s[:, fc2 * 256 + fc3 * 128:fc2 * 256 + (fc3 + 1) * 128],
                        h2bf[:, fc2 * 512:(fc2 + 1) * 512],
                        start=(fc2 == 0), stop=(fc2 == 3)))
                    r = p.op("tensor", fn, inc=("t" if fc2 == 3 else None))
                    th = r if r is not None else th
                p.wait("scalar", "t", th)
                fbank[pid] = p.op(
                    "scalar", lambda E, ps=ps, fc3=fc3: E.activation(
                        out=h3bf[:, fc3 * 512:(fc3 + 1) * 512], in_=ps[:],
                        func=AF.Gelu, bias=cb3[:, fc3:fc3 + 1]), inc="s")
            p.wait("tensor", "s", p.cnt["s"])
            th = None
            for fc3 in range(2):
                fn = (lambda E, fc3=fc3: E.matmul(
                    ps_w[0:1, 0:512], cnw4s[:, fc3:fc3 + 1],
                    h3bf[:, fc3 * 512:(fc3 + 1) * 512],
                    start=(fc3 == 0), stop=(fc3 == 1)))
                r = p.op("tensor", fn, inc=("t" if fc3 == 1 else None))
                th = r if r is not None else th
            p.wait("scalar", "t", th)
            fbank[0] = p.op("scalar", lambda E, tcs=tcs: E.activation(
                out=tfrow[0:1, tcs:tcs + 512], in_=ps_w[0:1, 0:512],
                func=AF.Sigmoid, bias=cb4[:]), inc="s")
        p.wait_cur("vector", "s")
        vc = p.op("vector", lambda E: E.tensor_scalar(
            out=crow[:], in0=tfrow[:], scalar1=bcoef[:], scalar2=acoef[:],
            op0=ALU.mult, op1=ALU.add), inc="v")
        p.wait("tensor", "v", vc)
        for tt in range(4):
            pid = tt % 2
            ps = ps_w if pid == 0 else ps_x
            p.wait("tensor", "s", fbank[pid])
            fn = (lambda E, tt=tt, ps=ps: E.matmul(
                ps[:], c["ones_col_f32"][:], crow[0:1, tt * 512:(tt + 1) * 512],
                start=True, stop=True))
            tb = p.op("tensor", fn, inc="t")
            p.wait("scalar", "t", tb)
            fbank[pid] = p.op("scalar", lambda E, ps=ps, tt=tt: E.activation(
                out=cbsb[:, tt * 512:(tt + 1) * 512], in_=ps[:],
                func=AF.Identity), inc="s")
        p.wait_cur("vector", "s")
        vtw = None
        for a, bb, opa, opb in PAIRS:
            for sl in range(4):
                s0 = sl * 512
                ca = slice(a * T + s0, a * T + s0 + 512)
                cb_ = slice(bb * T + s0, bb * T + s0 + 512)
                cc = slice(s0, s0 + 512)
                p.op("vector", lambda E, ca=ca, cb_=cb_, cc=cc: E.tensor_mul(
                    twu[:], cbsb[:, cc], outf[:, cb_]))
                p.op("vector", lambda E, ca=ca, cc=cc: E.tensor_mul(
                    twu2[:], cbsb[:, cc], outf[:, ca]))
                p.op("vector", lambda E, ca=ca, opa=opa: E.tensor_tensor(
                    out=outf[:, ca], in0=outf[:, ca], in1=twu[:], op=opa))
                r = p.op("vector", lambda E, cb_=cb_, opb=opb: E.tensor_tensor(
                    out=outf[:, cb_], in0=outf[:, cb_], in1=twu2[:], op=opb),
                    inc="v")
                vtw = r
        if cyc < 2:
            p.wait("scalar", "v", vtw)
            for jc in range(NJC):
                refresh_s = p.op("scalar", lambda E, jc=jc: E.activation(
                    out=outbf[:, jc * T:(jc + 1) * T],
                    in_=outf[:, jc * T:(jc + 1) * T], func=AF.Identity),
                    inc="s")

    cycle_end_t = p.cnt["t"]
    cycle_end_v = p.cnt["v"]
    cycle_end_s = p.cnt["s"]
    ictx.close()
    xold0 = ctx.enter_context(nc.sbuf_tensor("xold0_f", [128, 1024], F32))
    xold1 = ctx.enter_context(nc.sbuf_tensor("xold1_f", [128, 1024], F32))
    ln_alloc(nc, ctx, b, sfx="_f0")
    xout_re = xt_out.rearrange("(c p) t -> p c t", p=128)
    vr_hist, st_hist = [], []
    p.wait("sync", "t", cycle_end_t)
    p.wait("sync", "v", cycle_end_v)
    p.wait("sync", "s", cycle_end_s)
    for qc in range(NTC):
        xold = xold0 if qc % 2 == 0 else xold1
        ldsem = "ld" if qc % 2 == 0 else "lda"
        if qc >= 2:
            p.wait("sync", "v", vr_hist[qc - 2])
        fn = (lambda E, qc=qc, xold=xold: E.dma_start(
            out=xold[:].rearrange("p (c t) -> p c t", c=NJC),
            in_=xre[:, :, qc * 128:(qc + 1) * 128]))
        ld_x = p.op("sync", fn, inc=ldsem, amount=16)
        p.wait("vector", ldsem, ld_x)
        if qc >= 2:
            p.wait("vector", "st", st_hist[-1])
        if qc < 2:
            p.wait_cur("vector", "v")
            p.wait_cur("tensor", "s")
        vr, vx = emit_ln_epilogue(
            p, c, b, qc,
            lambda dc, qc=qc: outf[:, dc * T + qc * 128:dc * T + (qc + 1) * 128],
            ps_x if qc % 2 == 0 else ps_w, xold, bias_cols=None, xbf=None)
        vr_hist.append(vr)
        p.wait("sync", "v", vx)
        xn = b["scr_xn0"] if qc % 2 == 0 else b["scr_xn1"]
        fn = (lambda E, qc=qc, xn=xn: E.dma_start(
            out=xout_re[:, :, qc * 128:(qc + 1) * 128],
            in_=xn[:].rearrange("p (c t) -> p c t", c=NJC)))
        p.op("sync", fn, inc="st", amount=16)
        st_hist.append(p.cnt["st"])
    return b


def emit_l1_moe(p, c, ctx, xt_in, xt_out, w):
    nc = p.nc
    b = {}
    ictx = ExitStack()
    xf32 = ctx.enter_context(nc.sbuf_tensor("moe_xf32", [128, NJC * T], F32))
    gwf = ctx.enter_context(nc.sbuf_tensor("gwf", [128, NJC * 8], F32))
    gbrow = ctx.enter_context(nc.sbuf_tensor("gbrow", [1, 8], F32))
    pp = ctx.enter_context(nc.sbuf_tensor("pp", [128, 8], F32))
    ssum = ctx.enter_context(nc.sbuf_tensor("ssum", [128, 1], F32))
    rz1 = ctx.enter_context(nc.sbuf_tensor("rz1", [128, 1], F32))
    m1 = ctx.enter_context(nc.sbuf_tensor("m1g", [128, 1], F32))
    m2 = ctx.enter_context(nc.sbuf_tensor("m2g", [128, 1], F32))
    mr1 = ctx.enter_context(nc.sbuf_tensor("mr1g", [128, 8], F32))
    mr2 = ctx.enter_context(nc.sbuf_tensor("mr2g", [128, 8], F32))
    p1t = ctx.enter_context(nc.sbuf_tensor("p1tg", [128, 8], F32))
    p2t = ctx.enter_context(nc.sbuf_tensor("p2tg", [128, 8], F32))
    maskt = ctx.enter_context(nc.sbuf_tensor("masktg", [128, 8], F32))
    zs = ctx.enter_context(nc.sbuf_tensor("zsg", [128, 1], F32))
    rz2 = ctx.enter_context(nc.sbuf_tensor("rz2g", [128, 1], F32))
    cmb_all = ctx.enter_context(nc.sbuf_tensor("cmb_all", [128, NTC * 8], F32))
    crow = ctx.enter_context(nc.sbuf_tensor("crow_e", [1, T], F32))
    cbsb = ctx.enter_context(nc.sbuf_tensor("cbsb_e", [128, T], F32))
    s32s = [ctx.enter_context(nc.sbuf_tensor(f"s32_{i}", [128, 512], F32))
            for i in range(2)]
    t1s = ctx.enter_context(nc.sbuf_tensor("t1s", [128, 512], F32))
    xbf = ictx.enter_context(nc.sbuf_tensor("moe_xbf", [128, NJC * T], BF16))
    hbuf = ictx.enter_context(nc.sbuf_tensor("hbuf", [128, 16 * 1024], BF16))
    w1r = [ictx.enter_context(nc.sbuf_tensor(f"w1r{i}", [128, NJC * 256], BF16))
           for i in range(2)]
    w3r = [ictx.enter_context(nc.sbuf_tensor(f"w3r{i}", [128, NJC * 256], BF16))
           for i in range(2)]
    w2r = [ictx.enter_context(nc.sbuf_tensor(f"w2r{i}", [128, 16 * 128], BF16))
           for i in range(2)]
    ps_h = [ctx.enter_context(nc.psum_tensor(f"ps_h{i}", [128, 512], F32))
            for i in range(2)]
    ps_g = [ctx.enter_context(nc.psum_tensor(f"ps_g{i}", [128, 512], F32))
            for i in range(2)]
    ps_o = [ctx.enter_context(nc.psum_tensor(f"ps_o{i}", [128, 512], F32))
            for i in range(2)]
    ps_m = ctx.enter_context(nc.psum_tensor("ps_m", [128, 512], F32))

    xre = xt_in.rearrange("(c p) t -> p c t", p=128)
    xf3 = xf32[:].rearrange("p (c t) -> p c t", c=NJC)
    xbf3 = xbf[:].rearrange("p (c t) -> p c t", c=NJC)
    p.op("sync", lambda E: E.dma_start(out=xf3, in_=xre), inc="ld2", amount=16)
    p.op("gpsimd", lambda E: E.dma_start(out=xbf3, in_=xre), inc="wi", amount=16)
    gw3 = gwf[:].rearrange("p (c o) -> p c o", c=NJC)
    gsrc = w["gate_w"].rearrange("(c p) o -> p c o", p=128)
    p.op("sync", lambda E: E.dma_start(out=gw3, in_=gsrc), inc="ld2", amount=16)
    p.op("sync", lambda E: E.dma_start(out=gbrow[:], in_=w["gate_b"]),
         inc="ld2", amount=16)
    ld0 = p.cnt["ld2"]
    w_x = p.cnt["wi"]

    p.wait("tensor", "ld2", ld0)
    p.wait("tensor", "g", p.cnt["g"])
    for tc in range(NTC):
        p.wait_cur("tensor", "v")
        tg = None
        for jc in range(NJC):
            fn = (lambda E, tc=tc, jc=jc: E.matmul(
                ps_m[:, 0:8],
                xf32[:, jc * T + tc * 128:jc * T + (tc + 1) * 128],
                gwf[:, jc * 8:(jc + 1) * 8],
                start=(jc == 0), stop=False))
            p.op("tensor", fn)
        tg = p.op("tensor", lambda E: E.matmul(
            ps_m[:, 0:8], c["ones_col_f32"][:], gbrow[:],
            start=False, stop=True), inc="t")
        p.wait("vector", "t", tg)
        p.op("vector", lambda E: E.reduce_max(
            m1[:], ps_m[:, 0:8], axis=AX.X, negate=True))
        vn = p.op("vector", lambda E: E.tensor_copy(out=rz1[:], in_=m1[:]),
                  inc="v")
        p.wait("scalar", "v", vn)
        sg = p.op("scalar", lambda E: E.activation(
            out=pp[:], in_=ps_m[:, 0:8], func=AF.Exp, bias=rz1[:],
            accum_out=ssum[:]), inc="s")
        p.wait("vector", "s", sg)
        p.op("vector", lambda E: E.reciprocal(rz1[:], ssum[:]))
        p.op("vector", lambda E: E.tensor_scalar(
            out=pp[:], in0=pp[:], scalar1=rz1[:], scalar2=None, op0=ALU.mult))
        p.op("vector", lambda E: E.memset(mr1[:], -5.0))
        p.op("vector", lambda E: E.memset(mr2[:], -5.0))
        p.op("vector", lambda E: E.reduce_max(m1[:], pp[:], axis=AX.X))
        p.op("vector", lambda E: E.tensor_copy(out=mr1[:, 0:1], in_=m1[:]))
        p.op("vector", lambda E: E.match_replace(
            out=p1t[:], in_to_replace=mr1[:], in_values=pp[:], imm_value=-1.0))
        p.op("vector", lambda E: E.reduce_max(m2[:], p1t[:], axis=AX.X))
        p.op("vector", lambda E: E.tensor_copy(out=mr2[:, 0:1], in_=m2[:]))
        p.op("vector", lambda E: E.match_replace(
            out=p2t[:], in_to_replace=mr2[:], in_values=p1t[:], imm_value=-2.0))
        p.op("vector", lambda E: E.tensor_tensor(
            out=maskt[:], in0=pp[:], in1=p2t[:], op=ALU.not_equal))
        p.op("vector", lambda E: E.tensor_add(zs[:], m1[:], m2[:]))
        p.op("vector", lambda E: E.reciprocal(rz2[:], zs[:]))
        p.op("vector", lambda E: E.tensor_mul(maskt[:], maskt[:], pp[:]))
        vc = p.op("vector", lambda E, tc=tc: E.tensor_scalar(
            out=cmb_all[:, tc * 8:(tc + 1) * 8], in0=maskt[:], scalar1=rz2[:],
            scalar2=None, op0=ALU.mult), inc="v")
    gate_v = p.cnt["v"]

    p.wait("tensor", "wi", w_x)
    slot_tmm = {0: 0, 1: 0}
    ph_cons = {0: 0, 1: 0}
    po_cons = {0: 0, 1: 0}
    for e in range(8):
        p.wait_cur("tensor", "v")
        p.wait_cur("tensor", "s")
        for tcg in range(4):
            p.wait_cur("tensor", "s")
            tr = None
            for k in range(4):
                tc = tcg * 4 + k
                fn = (lambda E, tc=tc, k=k, e=e: E.matmul(
                    ps_m[0:1, k * 128:(k + 1) * 128],
                    cmb_all[:, tc * 8 + e:tc * 8 + e + 1], c["ident_f32"][:],
                    start=True, stop=True))
                r = p.op("tensor", fn, inc=("t" if k == 3 else None))
                tr = r if r is not None else tr
            p.wait("scalar", "t", tr)
            p.op("scalar", lambda E, tcg=tcg: E.activation(
                out=crow[0:1, tcg * 512:(tcg + 1) * 512], in_=ps_m[0:1, 0:512],
                func=AF.Identity), inc="s")
        for tt in range(4):
            p.wait_cur("tensor", "s")
            tb = p.op("tensor", lambda E, tt=tt: E.matmul(
                ps_m[:], c["ones_col_f32"][:], crow[0:1, tt * 512:(tt + 1) * 512],
                start=True, stop=True), inc="t")
            p.wait("scalar", "t", tb)
            p.op("scalar", lambda E, tt=tt: E.activation(
                out=cbsb[:, tt * 512:(tt + 1) * 512], in_=ps_m[:],
                func=AF.Identity), inc="s")
        cb_s = p.cnt["s"]
        p.wait("vector", "s", cb_s)

        for th in range(2):
            for fh in range(2):
                o_last_t = p.cnt["t"]
                p.wait("vector", "t", o_last_t)
                for fs in range(8):
                    slot = fs % 2
                    wsem = "wa" if slot == 0 else "wb"
                    p.wait("gpsimd", "t", slot_tmm[slot])
                    blk = fh * 8 + fs
                    ap1 = bass.AP(w["w1_t"], (e * 16 + blk) * 262144,
                                  [[2048, 128], [1, 2048]])
                    ap3 = bass.AP(w["w3_t"], (e * 16 + blk) * 262144,
                                  [[2048, 128], [1, 2048]])
                    p.op("gpsimd", lambda E, ap1=ap1, slot=slot: E.dma_start(
                        out=w1r[slot][:], in_=ap1), inc=wsem, amount=16)
                    wl = p.op("gpsimd", lambda E, ap3=ap3, slot=slot: E.dma_start(
                        out=w3r[slot][:], in_=ap3), inc=wsem, amount=16)
                    p.wait("tensor", wsem, wl)
                    for fc_l in range(2):
                        fc = fs * 2 + fc_l
                        for ts in range(2):
                            pidx = (fc_l * 2 + ts) % 2
                            p.wait("tensor", "v", ph_cons[pidx])
                            tcol = th * 1024 + ts * 512
                            th_ = None
                            for jc in range(NJC):
                                fn = (lambda E, jc=jc, fc_l=fc_l, tcol=tcol,
                                      slot=slot, pidx=pidx: E.matmul(
                                    ps_h[pidx][:],
                                    w1r[slot][:, jc * 256 + fc_l * 128:jc * 256 + (fc_l + 1) * 128],
                                    xbf[:, jc * T + tcol:jc * T + tcol + 512],
                                    start=(jc == 0), stop=(jc == NJC - 1)))
                                p.op("tensor", fn)
                            for jc in range(NJC):
                                fn = (lambda E, jc=jc, fc_l=fc_l, tcol=tcol,
                                      slot=slot, pidx=pidx: E.matmul(
                                    ps_g[pidx][:],
                                    w3r[slot][:, jc * 256 + fc_l * 128:jc * 256 + (fc_l + 1) * 128],
                                    xbf[:, jc * T + tcol:jc * T + tcol + 512],
                                    start=(jc == 0), stop=(jc == NJC - 1)))
                                r = p.op("tensor", fn,
                                         inc=("t" if jc == NJC - 1 else None))
                                th_ = r if r is not None else th_
                            slot_tmm[slot] = th_
                            p.wait("scalar", "t", th_)
                            p.wait("scalar", "v", ph_cons[pidx])
                            ss_ = p.op("scalar", lambda E, pidx=pidx: E.activation(
                                out=s32s[pidx][:], in_=ps_h[pidx][:], func=AF.Silu),
                                inc="s")
                            p.wait("vector", "s", ss_)
                            hcol = fc * 1024 + ts * 512
                            ph_cons[pidx] = p.op(
                                "vector", lambda E, hcol=hcol, pidx=pidx:
                                E.tensor_mul(hbuf[:, hcol:hcol + 512],
                                             s32s[pidx][:], ps_g[pidx][:]),
                                inc="v")
                h_done_v = p.cnt["v"]
                for dc in range(NJC):
                    slot = dc % 2
                    wsem = "wa" if slot == 0 else "wb"
                    p.wait("gpsimd", "t", slot_tmm[slot])
                    ap2 = bass.AP(w["w2_t"],
                                  e * 4194304 + dc * 524288 + fh * 2048,
                                  [[4096, 128], [1, 2048]])
                    wl = p.op("gpsimd", lambda E, ap2=ap2, slot=slot: E.dma_start(
                        out=w2r[slot][:], in_=ap2), inc=wsem, amount=16)
                    p.wait("tensor", wsem, wl)
                    if dc == 0:
                        p.wait("tensor", "v", h_done_v)
                    for ts in range(2):
                        pidx = (dc * 2 + ts) % 2
                        p.wait("tensor", "v", po_cons[pidx])
                        to_ = None
                        for fc in range(16):
                            fn = (lambda E, fc=fc, dc=dc, ts=ts, slot=slot,
                                  pidx=pidx: E.matmul(
                                ps_o[pidx][:],
                                w2r[slot][:, fc * 128:(fc + 1) * 128],
                                hbuf[:, fc * 1024 + ts * 512:fc * 1024 + ts * 512 + 512],
                                start=(fc == 0), stop=(fc == 15)))
                            r = p.op("tensor", fn, inc=("t" if fc == 15 else None))
                            to_ = r if r is not None else to_
                        slot_tmm[slot] = to_
                        p.wait("vector", "t", to_)
                        tcol = th * 1024 + ts * 512
                        p.op("vector", lambda E, tcol=tcol, pidx=pidx:
                             E.tensor_mul(t1s[:], ps_o[pidx][:],
                                          cbsb[:, tcol:tcol + 512]))
                        xcol = dc * T + tcol
                        po_cons[pidx] = p.op(
                            "vector", lambda E, xcol=xcol: E.tensor_add(
                                xf32[:, xcol:xcol + 512],
                                xf32[:, xcol:xcol + 512], t1s[:]), inc="v")

    experts_t = p.cnt["t"]
    experts_v = p.cnt["v"]
    experts_s = p.cnt["s"]
    ictx.close()
    ln_alloc(nc, ctx, b, need_r=False, sfx="_m1")
    p.wait("sync", "t", experts_t)
    p.wait("sync", "v", experts_v)
    p.wait("sync", "s", experts_s)
    xout_re = xt_out.rearrange("(c p) t -> p c t", p=128)
    st_hist = []
    for qc in range(NTC):
        if qc >= 2:
            p.wait("vector", "st", st_hist[-1])
        vr, vx = emit_ln_epilogue(
            p, c, b, qc,
            lambda dc, qc=qc: xf32[:, dc * T + qc * 128:dc * T + (qc + 1) * 128],
            ps_m if qc % 2 == 0 else ps_h[0], None, bias_cols=None, xbf=None)
        p.wait("sync", "v", vx)
        xn = b["scr_xn0"] if qc % 2 == 0 else b["scr_xn1"]
        fn = (lambda E, qc=qc, xn=xn: E.dma_start(
            out=xout_re[:, :, qc * 128:(qc + 1) * 128],
            in_=xn[:].rearrange("p (c t) -> p c t", c=NJC)))
        p.op("sync", fn, inc="st", amount=16)
        st_hist.append(p.cnt["st"])
    return b


def build_l1_moe():
    nc = bass.Bass("TRN2", target_bir_lowering=False)
    xt_in = nc.dram_tensor("xt_in", [D, T], F32, kind="ExternalInput").ap()
    gate_w = nc.dram_tensor("gate_w", [D, 8], F32, kind="ExternalInput").ap()
    gate_b = nc.dram_tensor("gate_b", [1, 8], F32, kind="ExternalInput").ap()
    w1_t = nc.dram_tensor("w1", [8, 16, 128, 8, 256], F32, kind="ExternalInput")
    w3_t = nc.dram_tensor("w3", [8, 16, 128, 8, 256], F32, kind="ExternalInput")
    w2_t = nc.dram_tensor("w2", [8, 8, 128, 32, 128], F32, kind="ExternalInput")
    xt_out = nc.dram_tensor("xt_out", [D, T], F32, kind="ExternalOutput").ap()
    with ExitStack() as ctx:
        p = Prog(nc, ctx)
        c = alloc_consts(nc, ctx)
        emit_consts(p, c)
        emit_l1_moe(p, c, ctx, xt_in, xt_out,
                    {"gate_w": gate_w, "gate_b": gate_b,
                     "w1_t": w1_t, "w3_t": w3_t, "w2_t": w2_t})
        finish(p)
        p.emit()
    return nc


def phase_barrier(p):
    for eng in ENGS:
        for sem in ("t", "v", "s", "g", "st", "ld", "lda", "ld2", "w", "wa", "wb", "wi"):
            p.wait_cur(eng, sem)


def build_fused():
    nc = bass.Bass("TRN2", target_bir_lowering=False)
    xt_in = nc.dram_tensor("xt_in", [D, T], F32, kind="ExternalInput").ap()
    wqkv = nc.dram_tensor("wqkv", [D, 3 * D], F32, kind="ExternalInput").ap()
    wproj = nc.dram_tensor("wproj", [D, D], F32, kind="ExternalInput").ap()
    ropec = nc.dram_tensor("ropec", [128, T], F32, kind="ExternalInput").ap()
    ropes = nc.dram_tensor("ropes", [128, T], F32, kind="ExternalInput").ap()
    wffn = {}
    for name, shape in (("cnw1", [1024, 1024]), ("cnw2", [1024, 512]),
                        ("cnw3", [512, 256]), ("cnw4", [256, 1]),
                        ("gcw1", [1024, 512]), ("gcw2", [512, 256]),
                        ("gcw3", [256, 1]),
                        ("cb1", [128, 8]), ("cb2", [128, 4]), ("cb3", [128, 2]),
                        ("cb4", [1, 1]), ("gb1", [128, 4]), ("gb2", [128, 2]),
                        ("gb3", [1, 1]), ("ar", [1, 1])):
        wffn[name] = nc.dram_tensor(name, shape, F32, kind="ExternalInput").ap()
    wqT = nc.dram_tensor("wqT", [D, D], F32, kind="ExternalInput").ap()
    wkT = nc.dram_tensor("wkT", [D, D], F32, kind="ExternalInput").ap()
    wvT = nc.dram_tensor("wvT", [D, D], F32, kind="ExternalInput").ap()
    woT = nc.dram_tensor("woT", [D, D], F32, kind="ExternalInput").ap()
    bq = nc.dram_tensor("bq", [128, 8], F32, kind="ExternalInput").ap()
    bk = nc.dram_tensor("bk", [128, 8], F32, kind="ExternalInput").ap()
    bo = nc.dram_tensor("bo", [128, 8], F32, kind="ExternalInput").ap()
    bv = nc.dram_tensor("bv", [1, D], F32, kind="ExternalInput").ap()
    gate_w = nc.dram_tensor("gate_w", [D, 8], F32, kind="ExternalInput").ap()
    gate_b = nc.dram_tensor("gate_b", [1, 8], F32, kind="ExternalInput").ap()
    w1_t = nc.dram_tensor("w1", [8, 16, 128, 8, 256], F32, kind="ExternalInput")
    w3_t = nc.dram_tensor("w3", [8, 16, 128, 8, 256], F32, kind="ExternalInput")
    w2_t = nc.dram_tensor("w2", [8, 8, 128, 32, 128], F32, kind="ExternalInput")
    xs1 = nc.dram_tensor("xs1", [D, T], F32, kind="Internal").ap()
    xs2 = nc.dram_tensor("xs2", [D, T], F32, kind="Internal").ap()
    xs3 = nc.dram_tensor("xs3", [D, T], F32, kind="Internal").ap()
    xt_out = nc.dram_tensor("xt_out", [D, T], F32, kind="ExternalOutput").ap()

    with ExitStack() as octx:
        p = Prog(nc, octx)
        c = alloc_consts(nc, octx)
        emit_consts(p, c)
        with ExitStack() as ctx1:
            emit_attention(p, c, ctx1, 0, xt_in, xs1,
                           {"wqkv": wqkv, "wproj": wproj},
                           rope={"cos": ropec, "sin": ropes})
            phase_barrier(p)
        with ExitStack() as ctx2:
            emit_l0_ffn(p, c, ctx2, xs1, xs2, wffn)
            phase_barrier(p)
        with ExitStack() as ctx3:
            emit_attention(p, c, ctx3, 1, xs2, xs3,
                           {"wqT": wqT, "wkT": wkT, "wvT": wvT, "woT": woT},
                           bias_aps={"bq": bq, "bk": bk, "bo": bo, "bv": bv})
            phase_barrier(p)
        with ExitStack() as ctx4:
            emit_l1_moe(p, c, ctx4, xs3, xt_out,
                        {"gate_w": gate_w, "gate_b": gate_b,
                         "w1_t": w1_t, "w3_t": w3_t, "w2_t": w2_t})
        finish(p)
        p.emit()
    return nc


def build_l0_ffn():
    nc = bass.Bass("TRN2", target_bir_lowering=False)
    xt_in = nc.dram_tensor("xt_in", [D, T], F32, kind="ExternalInput").ap()
    w = {}
    for name, shape in (("cnw1", [1024, 1024]), ("cnw2", [1024, 512]),
                        ("cnw3", [512, 256]), ("cnw4", [256, 1]),
                        ("gcw1", [1024, 512]), ("gcw2", [512, 256]),
                        ("gcw3", [256, 1]),
                        ("cb1", [128, 8]), ("cb2", [128, 4]), ("cb3", [128, 2]),
                        ("cb4", [1, 1]), ("gb1", [128, 4]), ("gb2", [128, 2]),
                        ("gb3", [1, 1]), ("ar", [1, 1])):
        w[name] = nc.dram_tensor(name, shape, F32, kind="ExternalInput").ap()
    xt_out = nc.dram_tensor("xt_out", [D, T], F32, kind="ExternalOutput").ap()
    with ExitStack() as ctx:
        p = Prog(nc, ctx)
        c = alloc_consts(nc, ctx)
        emit_consts(p, c)
        emit_l0_ffn(p, c, ctx, xt_in, xt_out, w)
        finish(p)
        p.emit()
    return nc


def finish(p):
    p.wait("sync", "st", p.cnt["st"])


def build_l0_attn():
    nc = bass.Bass("TRN2", target_bir_lowering=False)
    xt_in = nc.dram_tensor("xt_in", [D, T], F32, kind="ExternalInput").ap()
    wqkv = nc.dram_tensor("wqkv", [D, 3 * D], F32, kind="ExternalInput").ap()
    wproj = nc.dram_tensor("wproj", [D, D], F32, kind="ExternalInput").ap()
    ropec = nc.dram_tensor("ropec", [128, T], F32, kind="ExternalInput").ap()
    ropes = nc.dram_tensor("ropes", [128, T], F32, kind="ExternalInput").ap()
    xt_out = nc.dram_tensor("xt_out", [D, T], F32, kind="ExternalOutput").ap()

    with ExitStack() as ctx:
        p = Prog(nc, ctx)
        c = alloc_consts(nc, ctx)
        emit_consts(p, c)
        emit_attention(p, c, ctx, 0, xt_in, xt_out,
                       {"wqkv": wqkv, "wproj": wproj},
                       rope={"cos": ropec, "sin": ropes})
        finish(p)
        p.emit()
    return nc


def build_l1_attn():
    nc = bass.Bass("TRN2", target_bir_lowering=False)
    xt_in = nc.dram_tensor("xt_in", [D, T], F32, kind="ExternalInput").ap()
    wqT = nc.dram_tensor("wqT", [D, D], F32, kind="ExternalInput").ap()
    wkT = nc.dram_tensor("wkT", [D, D], F32, kind="ExternalInput").ap()
    wvT = nc.dram_tensor("wvT", [D, D], F32, kind="ExternalInput").ap()
    woT = nc.dram_tensor("woT", [D, D], F32, kind="ExternalInput").ap()
    bq = nc.dram_tensor("bq", [128, 8], F32, kind="ExternalInput").ap()
    bk = nc.dram_tensor("bk", [128, 8], F32, kind="ExternalInput").ap()
    bo = nc.dram_tensor("bo", [128, 8], F32, kind="ExternalInput").ap()
    bv = nc.dram_tensor("bv", [1, D], F32, kind="ExternalInput").ap()
    xt_out = nc.dram_tensor("xt_out", [D, T], F32, kind="ExternalOutput").ap()

    with ExitStack() as ctx:
        p = Prog(nc, ctx)
        c = alloc_consts(nc, ctx)
        emit_consts(p, c)
        emit_attention(p, c, ctx, 1, xt_in, xt_out,
                       {"wqT": wqT, "wkT": wkT, "wvT": wvT, "woT": woT},
                       bias_aps={"bq": bq, "bk": bk, "bo": bo, "bv": bv})
        finish(p)
        p.emit()
    return nc


def make_rope_tables():
    half = HD // 2
    theta = 1.0 / (10000.0 ** (np.arange(half, dtype=np.float64) / half))
    t = np.arange(T, dtype=np.float64)
    idx = np.arange(128) % 64
    freqs = np.outer(theta[idx // 2], t)
    cos = np.cos(freqs).astype(np.float32)
    sin = np.sin(freqs).astype(np.float32)
    sign = np.where(idx < 32, -1.0, 1.0).astype(np.float32)[:, None]
    return cos, sin * sign


def moe_perm13(w):
    w = np.asarray(w, dtype=np.float32)
    return np.ascontiguousarray(
        w.reshape(8, 8, 128, 16, 256).transpose(0, 3, 2, 1, 4))


def moe_perm2(w):
    w = np.asarray(w, dtype=np.float32)
    return np.ascontiguousarray(
        w.reshape(8, 32, 128, 8, 128).transpose(0, 3, 2, 1, 4))


def chunk_major(v):
    return np.ascontiguousarray(v.reshape(8, 128).T.astype(np.float32))


def kernel(**inputs):
    NCORES = 8
    f32c = lambda a: np.ascontiguousarray(a, dtype=np.float32)
    cos, sinS = make_rope_tables()
    shared = {
        "wqkv": f32c(inputs["L0_attn_w_qkv"]),
        "wproj": f32c(inputs["L0_attn_w_proj"]),
        "ropec": cos, "ropes": sinS,
        "cnw1": f32c(inputs["L0_ffn_cn_w1"]), "cnw2": f32c(inputs["L0_ffn_cn_w2"]),
        "cnw3": f32c(inputs["L0_ffn_cn_w3"]), "cnw4": f32c(inputs["L0_ffn_cn_w4"]),
        "gcw1": f32c(inputs["L0_ffn_gc_w1"]), "gcw2": f32c(inputs["L0_ffn_gc_w2"]),
        "gcw3": f32c(inputs["L0_ffn_gc_w3"]),
        "cb1": chunk_major(np.asarray(inputs["L0_ffn_cn_b1"])),
        "cb2": f32c(np.asarray(inputs["L0_ffn_cn_b2"]).reshape(4, 128).T),
        "cb3": f32c(np.asarray(inputs["L0_ffn_cn_b3"]).reshape(2, 128).T),
        "cb4": f32c(np.asarray(inputs["L0_ffn_cn_b4"]).reshape(1, 1)),
        "gb1": f32c(np.asarray(inputs["L0_ffn_gc_b1"]).reshape(4, 128).T),
        "gb2": f32c(np.asarray(inputs["L0_ffn_gc_b2"]).reshape(2, 128).T),
        "gb3": f32c(np.asarray(inputs["L0_ffn_gc_b3"]).reshape(1, 1)),
        "ar": f32c(np.asarray(inputs["L0_ffn_adaptive_range"]).reshape(1, 1)),
        "wqT": f32c(np.asarray(inputs["L1_attn_Wq"]).T),
        "wkT": f32c(np.asarray(inputs["L1_attn_Wk"]).T),
        "wvT": f32c(np.asarray(inputs["L1_attn_Wv"]).T),
        "woT": f32c(np.asarray(inputs["L1_attn_Wo"]).T),
        "bq": chunk_major(np.asarray(inputs["L1_attn_bq"])),
        "bk": chunk_major(np.asarray(inputs["L1_attn_bk"])),
        "bo": chunk_major(np.asarray(inputs["L1_attn_bo"])),
        "bv": f32c(np.asarray(inputs["L1_attn_bv"]).reshape(1, D)),
        "gate_w": f32c(inputs["L1_ffn_gate_w"]),
        "gate_b": f32c(np.asarray(inputs["L1_ffn_gate_b"]).reshape(1, 8)),
        "w1": moe_perm13(inputs["L1_ffn_w1"]),
        "w3": moe_perm13(inputs["L1_ffn_w3"]),
        "w2": moe_perm2(inputs["L1_ffn_w2"]),
    }
    x0 = inputs["L0_attn_x"]
    nc = build_fused()
    in_maps = [dict(shared, xt_in=f32c(x0[b].T)) for b in range(NCORES)]
    res = run_bass_kernel_spmd(nc, in_maps, core_ids=list(range(NCORES)))
    return np.stack([np.asarray(r["xt_out"], np.float32).T
                     for r in res.results], axis=0)
